# version 02 (final)
# baseline (speedup 1.0000x reference)
import numpy as np
import ml_dtypes
import concourse.bass as bass
import concourse.mybir as mybir
from concourse.bass_utils import run_bass_kernel_spmd
from concourse.alu_op_type import AluOpType as OP

F32 = mybir.dt.float32
F16 = mybir.dt.float16
I32 = mybir.dt.int32
I8 = mybir.dt.int8
BF16 = mybir.dt.bfloat16
AF = mybir.ActivationFunctionType
AX = mybir.AxisListType.X

D = 1024
S = 2048
HEADS = 16
DH = 64
FH = 3072
EC = 8
HC = 24
NG = 4
GT = 512
NQT = 16
NCH = 16
SEQ = 2
RING = 6
EPS = 1e-5
SCL = 0.125
CST = 12.0
FS = 1.0 / 127.0
H_SH = 2.0 ** -5
FFN_OUT_SCALE = FS * FS / H_SH

N_SUB = 4
DEV_STAGE = 5
DEV_NTILES = 99
DEV_NPAIRS = 99
DEV_TP = 9

ENGS = ("tensor", "vector", "scalar", "gpsimd", "sync")
SWAP_MASK = [i + 1 if i % 2 == 0 else i - 1 for i in range(32)]


class Plan:
    def __init__(self):
        self.plan = {e: [] for e in ENGS}
        self.h = {}
        self.cnt = {}
        self.lastwait = {e: {} for e in ENGS}

    def add_sem(self, name, handle):
        self.h[name] = handle
        self.cnt[name] = 0

    def op(self, eng, fn, inc=None, n=1):
        if inc is None:
            self.plan[eng].append(fn)
            return None
        self.cnt[inc] += n
        h = self.h[inc]
        self.plan[eng].append(lambda e, fn=fn, h=h, n=n: fn(e).then_inc(h, n))
        return self.cnt[inc]

    def wait(self, eng, sem, val=None):
        if val is None:
            val = self.cnt[sem]
        if val <= 0:
            return
        lw = self.lastwait[eng]
        if lw.get(sem, 0) >= val:
            return
        lw[sem] = val
        h = self.h[sem]
        self.plan[eng].append(lambda e, h=h, v=val: e.wait_ge(h, v))

    def barrier(self):
        vals = {}
        for e in ENGS:
            self.cnt["b_" + e] += 1
            h = self.h["b_" + e]
            if e != "sync":
                self.plan[e].append(lambda en: en.drain())
            self.plan[e].append(lambda en, h=h: en.nop().then_inc(h, 1))
            vals[e] = self.cnt["b_" + e]
        for e in ENGS:
            for o in ENGS:
                if o != e:
                    self.wait(e, "b_" + o, vals[o])


class Res:
    def __init__(self, p):
        self.p = p
        self.ready = None
        self.users = []

    def write_by(self, sem):
        self.ready = (sem, self.p.cnt[sem])

    def wait_ready(self, eng):
        if self.ready:
            self.p.wait(eng, *self.ready)

    def read_by(self, sem):
        self.users.append((sem, self.p.cnt[sem]))

    def wait_free(self, eng):
        for s, v in self.users:
            self.p.wait(eng, s, v)
        self.users = []
        if self.ready:
            self.p.wait(eng, *self.ready)


class Arena:
    def __init__(self, ap, nbytes):
        self.ap = ap
        self.nbytes = nbytes
        self.off = 0

    def take(self, cols, dt):
        sz = cols * mybir.dt.size(dt)
        off = -(-self.off // 32) * 32
        self.off = off + sz
        assert self.off <= self.nbytes, f"arena overflow {self.off} > {self.nbytes}"
        return self.ap[:, off:off + sz].bitcast(dt)

    def mark(self):
        return self.off

    def release(self, mark):
        self.off = mark


ARENA_BYTES = 108 * 1024


class Ctx:
    pass


def _mm(c, ps, lhsT, rhs, start, stop, inc=None):
    c.p.op("tensor",
           lambda e, a=ps, b=lhsT, r=rhs, s1=start, s2=stop:
           e.matmul(a, b, r, start=s1, stop=s2), inc=inc)


def mm3(c, ps, lh, ll, rh, rl, start, stop, inc=None):
    _mm(c, ps, lh, rh, start, False)
    _mm(c, ps, ll, rh, False, False)
    _mm(c, ps, lh, rl, False, stop, inc=inc)


def mm2(c, ps, lw, rh, rl, start, stop, inc=None):
    _mm(c, ps, lw, rh, start, False)
    _mm(c, ps, lw, rl, False, stop, inc=inc)


def dma(c, qeng, sem, out, in_, res_list=()):
    for r in res_list:
        r.wait_free(qeng)
    c.p.op(qeng, lambda e, o=out, i=in_: e.dma_start(out=o, in_=i), inc=sem, n=16)
    return c.p.cnt[sem]


def v_op(c, fn):
    c.p.op("vector", fn)
    c.p.op("vector", lambda e: e.drain(), inc="v")
    return c.p.cnt["v"]


def s_op(c, fn):
    c.p.op("scalar", fn)
    c.p.op("scalar", lambda e: e.drain(), inc="s")
    return c.p.cnt["s"]


def g_op(c, fn):
    c.p.op("gpsimd", fn)
    c.p.op("gpsimd", lambda e: e.drain(), inc="g")
    return c.p.cnt["g"]


def t_nop(c):
    c.p.op("tensor", lambda e: e.nop(), inc="t")
    return c.p.cnt["t"]


def bcast16(c, ps_out, row_f32, rowh, rowl, ones_slice, inc="t"):
    p = c.p
    p.wait("vector", "t")
    v_op(c, lambda e, o=rowh, i=row_f32: e.tensor_copy(o, i))
    vv = v_op(c, lambda e, o=rowl, a=row_f32, b=rowh: e.tensor_tensor(o, a, b, OP.subtract))
    p.wait("tensor", "v", vv)
    _mm(c, ps_out, ones_slice, rowh, True, False)
    _mm(c, ps_out, ones_slice, rowl, False, True, inc=inc)
    return p.cnt["t"]



def build_program(n_sub=N_SUB):
    nc = bass.Bass("TRN2", target_bir_lowering=False)
    nc.detect_race_conditions = False
    p = Plan()
    c = Ctx()
    c.nc = nc
    c.p = p

    def din(name, shape, dt):
        return nc.dram_tensor(name, shape, dt, kind="ExternalInput").ap()

    x_in = din("x", [SEQ, 128, EC * S], F32)
    y_out = nc.dram_tensor("y", [SEQ, 128, EC * S], F32, kind="ExternalOutput").ap()

    drn = dict(
        x=x_in, y=y_out,
        wq=din("wq", [EC, 128, D], F32),
        wk=din("wk", [EC, 128, D], F32),
        wv=din("wv", [EC, 128, D], F32),
        wo=din("wo", [EC, 128, D], F32),
        wqkv=din("wqkv", [3 * EC, 128, D], F32),
        wout=din("wout", [EC, 128, D], F32),
        biases=din("biases", [128, 5 * EC], F32),
        wgi=din("wgi", [HC, 128, D], I32),
        wui=din("wui", [HC, 128, D], I32),
        wdi=din("wdi", [EC, 128, FH], I32),
        m_mid=din("m_mid", [128, 384], F16),
        m_t0=din("m_t0", [128, 256], F16),
        m_t1=din("m_t1", [128, 384], F16),
        m_b=din("m_b", [128, 896], F16),
        rc2=din("rc2", [128, S], F32),
        rs2=din("rs2", [128, S], F32),
        id32=din("id32", [128, 128], F32),
        scr_g=nc.dram_tensor("scr_g", [HC, 128, D], F16, kind="Internal").ap(),
        scr_u=nc.dram_tensor("scr_u", [HC, 128, D], F16, kind="Internal").ap(),
        scr_d=nc.dram_tensor("scr_d", [EC, 128, FH], F16, kind="Internal").ap(),
        scr_xnh=nc.dram_tensor("scr_xnh", [NG, 128, EC * GT], F16, kind="Internal").ap(),
        scr_xnl=nc.dram_tensor("scr_xnl", [NG, 128, EC * GT], F16, kind="Internal").ap(),
        scr_aoh=nc.dram_tensor("scr_aoh", [EC, 128, S], F16, kind="Internal").ap(),
        scr_aol=nc.dram_tensor("scr_aol", [EC, 128, S], F16, kind="Internal").ap(),
    )
    c.dram = drn

    import contextlib
    stack = contextlib.ExitStack()

    def sbt(name, shape, dt):
        return stack.enter_context(nc.sbuf_tensor(name, shape, dt))[:]

    with stack:
        c.sb = dict(
            xfm=sbt("xfm", [128, EC * S], F32),
            xnh=sbt("xnh", [128, EC * GT], F16),
            xnl=sbt("xnl", [128, EC * GT], F16),
            wb32=sbt("wb32", [128, D], F32),
            wph=sbt("wph", [128, 2 * D], F16),
            wpl=sbt("wpl", [128, 2 * D], F16),
            m_mid=sbt("s_mmid", [128, 384], F16),
            m_t0=sbt("s_mt0", [128, 256], F16),
            m_t1=sbt("s_mt1", [128, 384], F16),
            m_b=sbt("s_mb", [128, 896], F16),
            id32=sbt("s_id32", [128, 128], F32),
            bias=sbt("s_bias", [128, 5 * EC], F32),
            ones_c=sbt("ones_c", [128, 1], F32),
            ones_r16=sbt("ones_r16", [1, 128], F16),
            negC=sbt("negC", [128, 1], F32),
            epsc=sbt("epsc", [128, 1], F32),
        )
        arena_t = sbt("arena", [128, ARENA_BYTES], I8)

        c.psum = [stack.enter_context(nc.psum_tensor(f"pb{i}", [128, 512], F32))[:]
                  for i in range(8)]

        for name in ["lx", "lw", "lg0", "lg1", "lu0", "lu1", "ldn0", "ldn1",
                     "lp0", "lp1", "st", "t", "v", "s", "g"] + ["b_" + e for e in ENGS]:
            p.add_sem(name, stack.enter_context(nc.semaphore(name)))

        c.bank = [Res(p) for _ in range(8)]
        c.arena = Arena(arena_t, ARENA_BYTES)
        c.wslot = [Res(p), Res(p)]
        c.xn_res = Res(p)
        c.misc = {}

        emit_program(c, n_sub)

        with nc.Block() as block:
            for eng in ENGS:
                getattr(block, eng)(lambda e, eng=eng: [f(e) for f in p.plan[eng]])

    return nc


def emit_program(c, n_sub):
    p = c.p
    for name in ["m_mid", "m_t0", "m_t1", "m_b", "id32"]:
        dma(c, "sync", "lx", c.sb[name], c.dram[name])
    dma(c, "sync", "lx", c.sb["bias"], c.dram["biases"])
    v_op(c, lambda e: e.memset(c.sb["ones_c"], 1.0))
    v_op(c, lambda e: e.memset(c.sb["ones_r16"], 1.0))
    v_op(c, lambda e: e.memset(c.sb["negC"], -CST))
    v_op(c, lambda e: e.memset(c.sb["epsc"], EPS))

    ffn_prep(c)
    p.barrier()

    for seq in range(SEQ):
        ldv = dma(c, "sync", "lx", c.sb["xfm"], c.dram["x"][seq])
        for eng in ("vector", "scalar", "tensor", "gpsimd"):
            p.wait(eng, "lx", ldv)
        p.barrier()

        if n_sub >= 1:
            sub_attn_a(c, seq)
            p.barrier()
        if n_sub >= 2:
            sub_ffn(c, seq)
            p.barrier()
        if n_sub >= 3:
            sub_attn_b(c, seq)
            p.barrier()
        if n_sub >= 4:
            sub_ffn(c, seq)
            p.barrier()

        dma(c, "scalar", "st", c.dram["y"][seq], c.sb["xfm"])
        p.barrier()

    p.wait("sync", "st")
    p.wait("gpsimd", "st")


def ffn_prep(c):
    p = c.p
    m = c.arena.mark()
    bufs = []
    for i in range(2):
        bi = c.arena.take(FH, I32)
        bf = c.arena.take(FH, F16)
        bufs.append((bi, bf, Res(p), Res(p)))
    jobs = []
    for sl in range(HC):
        jobs.append((c.dram["wgi"][sl], c.dram["scr_g"][sl], D))
        jobs.append((c.dram["wui"][sl], c.dram["scr_u"][sl], D))
    for sl in range(EC):
        jobs.append((c.dram["wdi"][sl], c.dram["scr_d"][sl], FH))
    for j, (src, dst, n) in enumerate(jobs):
        bi, bf, r_i, r_f = bufs[j % 2]
        ldv = dma(c, "sync", "lp" + str(j % 2), bi[:, 0:n], src, res_list=[r_i])
        p.wait("vector", "lp" + str(j % 2), ldv)
        p.wait("vector", "st")
        v_op(c, lambda e, o=bf[:, 0:n], i=bi[:, 0:n]: e.tensor_copy(o, i))
        r_i.read_by("v")
        r_f.write_by("v")
        r_f.wait_ready("scalar")
        dma(c, "scalar", "st", dst, bf[:, 0:n])
    c.arena.release(m)



def layer_norm_group(c, g, wk):
    p = c.p
    xfm = c.sb["xfm"]
    b5, b6 = c.psum[5], c.psum[6]
    r5, r6 = c.bank[5], c.bank[6]
    xsq, t1, xnf = wk["xsq"], wk["t1"], wk["xnf"]
    mrow, trow = wk["mrow"], wk["trow"]
    rowh, rowl = wk["rowh"], wk["rowl"]
    xsq_res = wk["xsq_res"]
    r5.wait_free("tensor")
    r6.wait_free("tensor")
    for dc in range(EC):
        xc = xfm[:, dc * S + g * GT: dc * S + g * GT + GT]
        xsq_res.wait_free("scalar")
        sv = s_op(c, lambda e, o=xsq, i=xc: e.activation(o, i, AF.Square, bias=0.0, scale=1.0))
        xsq_res.write_by("s")
        _mm(c, b5[0:1, :], c.sb["ones_c"], xc, dc == 0, dc == EC - 1)
        p.wait("tensor", "s", sv)
        _mm(c, b6[0:1, :], c.sb["ones_c"], xsq, dc == 0, dc == EC - 1, inc="t")
        xsq_res.read_by("t")
    p.wait("vector", "t")
    v_op(c, lambda e, o=mrow, i=b5[0:1, :]: e.tensor_scalar(o, i, 1.0 / D, None, OP.mult))
    v_op(c, lambda e, o=trow, i=b6[0:1, :]: e.tensor_scalar(o, i, 1.0 / D, None, OP.mult))
    r5.read_by("v")
    r6.read_by("v")
    v_op(c, lambda e, o=xsq[0:1, :], a=mrow, b=mrow: e.tensor_tensor(o, a, b, OP.mult))
    vv = v_op(c, lambda e, o=trow, a=trow, b=xsq[0:1, :]: e.tensor_tensor(o, a, b, OP.subtract))
    p.wait("scalar", "v", vv)
    sv = s_op(c, lambda e, o=trow, i=trow: e.activation(o, i, AF.Sqrt, bias=c.sb["epsc"][0:1, :], scale=1.0))
    p.wait("vector", "s", sv)
    v_op(c, lambda e, o=trow, i=trow: e.reciprocal(o, i))
    v_op(c, lambda e, o=mrow, a=mrow, b=trow: e.tensor_tensor(o, a, b, OP.mult))
    r5.wait_free("tensor")
    r6.wait_free("tensor")
    bcast16(c, b5, mrow, rowh, rowl, c.sb["ones_r16"])
    bcast16(c, b6, trow, rowh, rowl, c.sb["ones_r16"])
    p.wait("vector", "t")
    c.xn_res.wait_free("vector")
    for dc in range(EC):
        xc = xfm[:, dc * S + g * GT: dc * S + g * GT + GT]
        oh = c.sb["xnh"][:, dc * GT:(dc + 1) * GT]
        ol = c.sb["xnl"][:, dc * GT:(dc + 1) * GT]
        v_op(c, lambda e, o=t1, a=xc, b=b6: e.tensor_tensor(o, a, b, OP.mult))
        v_op(c, lambda e, o=xnf, a=t1, b=b5: e.tensor_tensor(o, a, b, OP.subtract))
        v_op(c, lambda e, o=oh, i=xnf: e.tensor_copy(o, i))
        v_op(c, lambda e, o=ol, a=xnf, b=oh: e.tensor_tensor(o, a, b, OP.subtract))
    r5.read_by("v")
    r6.read_by("v")
    c.xn_res.write_by("v")


def ln_wk(c, a):
    return dict(
        xsq=a.take(GT, F32), t1=a.take(GT, F32), xnf=a.take(GT, F32),
        mrow=a.take(GT, F32)[0:1, :], trow=a.take(GT, F32)[0:1, :],
        rowh=a.take(GT, F16)[0:1, :], rowl=a.take(GT, F16)[0:1, :],
        xsq_res=Res(c.p),
    )


def stream_w(c, src_slice, slot_idx):
    p = c.p
    slot = c.wslot[slot_idx]
    w32 = c.sb["wb32"]
    w32_res = c.misc.setdefault("w32_res", Res(p))
    wh = c.sb["wph"][:, slot_idx * D:(slot_idx + 1) * D]
    wl = c.sb["wpl"][:, slot_idx * D:(slot_idx + 1) * D]
    ldv = dma(c, "sync", "lw", w32, src_slice, res_list=[w32_res])
    w32_res.write_by("lw")
    p.wait("vector", "lw", ldv)
    slot.wait_free("vector")
    v_op(c, lambda e, o=wh, i=w32: e.tensor_copy(o, i))
    v_op(c, lambda e, o=wl, a=w32, b=wh: e.tensor_tensor(o, a, b, OP.subtract))
    w32_res.read_by("v")
    slot.write_by("v")
    return wh, wl, slot



def sub_attn_a(c, seq):
    p = c.p
    a = c.arena
    m0 = a.mark()
    xfm = c.sb["xfm"]
    b = c.psum
    rb = c.bank
    bias = c.sb["bias"]

    krh = a.take(RING * D, F16)
    krl = a.take(RING * D, F16)
    vrh = a.take(RING * 1040, F16)
    vrl = a.take(RING * 1040, F16)
    qh = a.take(EC * GT, F16)
    ql = a.take(EC * GT, F16)
    aoh = a.take(EC * GT, F16)
    aol = a.take(EC * GT, F16)
    kgh = a.take(EC * 4, F16)
    kgl = a.take(EC * 4, F16)
    vgh = a.take(1040, F16)
    vgl = a.take(1040, F16)
    qgh = a.take(EC * 4, F16)
    qgl = a.take(EC * 4, F16)
    aogh = a.take(EC * 4, F16)
    aogl = a.take(EC * 4, F16)
    wk = ln_wk(c, a)
    v32 = a.take(GT, F32)
    p0 = a.take(GT, F32)
    pph = [a.take(GT, BF16) for _ in range(2)]
    ppl = [a.take(GT, BF16) for _ in range(2)]
    ao32 = a.take(128, F32)
    rrow = a.take(128, F32)[0:1, :]
    rr16h = a.take(128, BF16)[0:1, :]
    rr16l = a.take(128, BF16)[0:1, :]
    nsc = a.take(256, F16)
    aga = a.take(64, F32)

    ring_k = [Res(p) for _ in range(RING)]
    ring_v = [Res(p) for _ in range(RING)]
    q_res = Res(p)
    ao_res = Res(p)
    w_v32 = Res(p)
    w_p0 = Res(p)
    pp_res = [Res(p), Res(p)]
    w_ao32 = Res(p)
    w_rrow = Res(p)
    aga_res = Res(p)

    v_op(c, lambda e, o=aga: e.memset(o, 0.0))
    aga_res.write_by("v")

    def proj_group(g, mats, slot0):
        si = slot0
        for mat, wsrc, bcol in mats:
            for ec in range(EC):
                wh, wl, slot = stream_w(c, wsrc[ec], si % 2)
                si += 1
                rb[0].wait_free("tensor")
                slot.wait_ready("tensor")
                c.xn_res.wait_ready("tensor")
                for dc in range(EC):
                    mm3(c, b[0], wh[:, dc * 128:dc * 128 + 128],
                        wl[:, dc * 128:dc * 128 + 128],
                        c.sb["xnh"][:, dc * GT:(dc + 1) * GT],
                        c.sb["xnl"][:, dc * GT:(dc + 1) * GT],
                        dc == 0, dc == EC - 1,
                        inc=("t" if dc == EC - 1 else None))
                slot.read_by("t")
                c.xn_res.read_by("t")
                bslc = bias[:, bcol * EC + ec: bcol * EC + ec + 1]
                p.wait("vector", "t")
                if mat == "q":
                    q_res.wait_free("vector")
                    oh = qh[:, ec * GT:(ec + 1) * GT]
                    ol = ql[:, ec * GT:(ec + 1) * GT]
                    v_op(c, lambda e, o=oh, i=b[0], s1=bslc:
                         e.tensor_scalar(o, i, s1, None, OP.add))
                    v_op(c, lambda e, o=ol, i=b[0], s1=bslc, hh=oh:
                         e.scalar_tensor_tensor(o, i, s1, hh, OP.add, OP.subtract))
                    rb[0].read_by("v")
                    q_res.write_by("v")
                elif mat == "k":
                    for tc in range(4):
                        sl = (g * 4 + tc) % RING
                        ring_k[sl].wait_free("vector")
                        oh = krh[:, sl * D + ec * 128: sl * D + ec * 128 + 128]
                        ol = krl[:, sl * D + ec * 128: sl * D + ec * 128 + 128]
                        ib = b[0][:, tc * 128:(tc + 1) * 128]
                        v_op(c, lambda e, o=oh, i=ib, s1=bslc:
                             e.tensor_scalar(o, i, s1, None, OP.add))
                        v_op(c, lambda e, o=ol, i=ib, s1=bslc, hh=oh:
                             e.scalar_tensor_tensor(o, i, s1, hh, OP.add, OP.subtract))
                        ring_k[sl].write_by("v")
                    rb[0].read_by("v")
                else:
                    w_v32.wait_free("vector")
                    v_op(c, lambda e, o=v32, i=b[0], s1=bslc:
                         e.tensor_scalar(o, i, s1, None, OP.add))
                    rb[0].read_by("v")
                    w_v32.write_by("v")
                    for tc in range(4):
                        sl = (g * 4 + tc) % RING
                        rb[1].wait_free("tensor")
                        w_v32.wait_ready("tensor")
                        c.p.op("tensor", lambda e, o=b[1][:, 0:128],
                               i=v32[:, tc * 128:(tc + 1) * 128], idn=c.sb["id32"]:
                               e.transpose(o, i, idn), inc="t")
                        w_v32.read_by("t")
                        p.wait("vector", "t")
                        ring_v[sl].wait_free("vector")
                        for h2 in range(2):
                            head = ec * 2 + h2
                            oh = vrh[:, sl * 1040 + head * 65: sl * 1040 + head * 65 + 64]
                            ol = vrl[:, sl * 1040 + head * 65: sl * 1040 + head * 65 + 64]
                            ib = b[1][:, h2 * 64:(h2 + 1) * 64]
                            v_op(c, lambda e, o=oh, i=ib: e.tensor_copy(o, i))
                            v_op(c, lambda e, o=ol, aa=ib, bb=oh:
                                 e.tensor_tensor(o, aa, bb, OP.subtract))
                        rb[1].read_by("v")
                        if ec == EC - 1:
                            onesh = vrh[:, sl * 1040: sl * 1040 + 1040] \
                                .rearrange("p (h cc) -> p h cc", cc=65)[:, :, 64:65]
                            onesl = vrl[:, sl * 1040: sl * 1040 + 1040] \
                                .rearrange("p (h cc) -> p h cc", cc=65)[:, :, 64:65]
                            v_op(c, lambda e, o=onesh: e.memset(o, 1.0))
                            v_op(c, lambda e, o=onesl: e.memset(o, 0.0))
                            ring_v[sl].write_by("v")
        return si

    def attn_tile(qt):
        qcol = (qt % 4) * 128
        cs = [ch for ch in (qt - 1, qt, qt + 1) if 0 <= ch < NCH]
        ncs = len(cs)
        if qt == 0:
            msk = c.sb["m_t0"]
        elif qt == 1:
            msk = c.sb["m_t1"]
        elif qt == NQT - 1:
            msk = c.sb["m_mid"][:, 0:256]
        else:
            msk = c.sb["m_mid"]
        use_glob = qt >= 2
        for pair in range(min(EC, DEV_NPAIRS)):
            for h2 in range(2):
                head = pair * 2 + h2
                sbanks = (1, 2) if h2 == 0 else (3, 4)
                abank = 5 + h2
                ra_ = rb[abank]
                q_res.wait_ready("tensor")
                qhs = qh[h2 * 64:(h2 + 1) * 64, pair * GT + qcol: pair * GT + qcol + 128]
                qls = ql[h2 * 64:(h2 + 1) * 64, pair * GT + qcol: pair * GT + qcol + 128]
                w_p0.wait_free("scalar")
                for ci, ch in enumerate(cs):
                    sl = ch % RING
                    bk = sbanks[ci % 2]
                    rb[bk].wait_free("tensor")
                    ring_k[sl].wait_ready("tensor")
                    kb_h = krh[h2 * 64:(h2 + 1) * 64,
                               sl * D + pair * 128: sl * D + pair * 128 + 128]
                    kb_l = krl[h2 * 64:(h2 + 1) * 64,
                               sl * D + pair * 128: sl * D + pair * 128 + 128]
                    mm3(c, b[bk][:, 0:128], kb_h, kb_l, qhs, qls, True, True, inc="t")
                    ring_k[sl].read_by("t")
                    q_res.read_by("t")
                    p.wait("scalar", "t")
                    s_op(c, lambda e, o=p0[:, ci * 128:(ci + 1) * 128], i=b[bk][:, 0:128]:
                         e.activation(o, i, AF.Exp, bias=c.sb["negC"], scale=SCL))
                    rb[bk].read_by("s")
                if use_glob:
                    bk = sbanks[ncs % 2]
                    rb[bk].wait_free("tensor")
                    c.misc["kg_res"].wait_ready("tensor")
                    kgh_s = kgh[h2 * 64:(h2 + 1) * 64, pair * 4: pair * 4 + 4]
                    kgl_s = kgl[h2 * 64:(h2 + 1) * 64, pair * 4: pair * 4 + 4]
                    mm3(c, b[bk][0:4, 0:128], kgh_s, kgl_s, qhs, qls, True, True, inc="t")
                    q_res.read_by("t")
                    p.wait("scalar", "t")
                    s_op(c, lambda e, o=p0[0:4, 384:512], i=b[bk][0:4, 0:128]:
                         e.activation(o, i, AF.Exp, bias=c.sb["negC"][0:4, :], scale=SCL))
                    rb[bk].read_by("s")
                w_p0.write_by("s")
                sv = p.cnt["s"]
                p.wait("gpsimd", "s", sv)
                g_op(c, lambda e, o=p0[:, 0:ncs * 128], aa=p0[:, 0:ncs * 128], bm=msk:
                     e.tensor_tensor(o, aa, bm, OP.mult))
                w_p0.write_by("g")
                gv = p.cnt["g"]
                ppr = pp_res[h2]
                ppr.wait_free("vector")
                p.wait("vector", "g", gv)
                hi_cols = ncs * 128
                v_op(c, lambda e, o=pph[h2][:, 0:hi_cols], i=p0[:, 0:hi_cols]:
                     e.tensor_copy(o, i))
                v_op(c, lambda e, o=ppl[h2][:, 0:hi_cols], aa=p0[:, 0:hi_cols],
                     bb=pph[h2][:, 0:hi_cols]: e.tensor_tensor(o, aa, bb, OP.subtract))
                if use_glob:
                    v_op(c, lambda e, o=pph[h2][0:4, 384:512], i=p0[0:4, 384:512]:
                         e.tensor_copy(o, i))
                    v_op(c, lambda e, o=ppl[h2][0:4, 384:512], aa=p0[0:4, 384:512],
                         bb=pph[h2][0:4, 384:512]: e.tensor_tensor(o, aa, bb, OP.subtract))
                w_p0.read_by("v")
                ppr.write_by("v")
                vv = p.cnt["v"]
                ra_.wait_free("tensor")
                p.wait("tensor", "v", vv)
                for oi, (ci, ch) in enumerate(list(enumerate(cs))):
                    sl = ch % RING
                    ring_v[sl].wait_ready("tensor")
                    vt_h = vrh[:, sl * 1040 + head * 65: sl * 1040 + head * 65 + 65]
                    vt_l = vrl[:, sl * 1040 + head * 65: sl * 1040 + head * 65 + 65]
                    last = (not use_glob) and oi == ncs - 1
                    mm3(c, b[abank][0:65, 0:128], vt_h, vt_l,
                        pph[h2][:, ci * 128:(ci + 1) * 128],
                        ppl[h2][:, ci * 128:(ci + 1) * 128],
                        oi == 0, last, inc=("t" if last else None))
                if use_glob:
                    c.misc["vg_res"].wait_ready("tensor")
                    mm3(c, b[abank][0:65, 0:128],
                        vgh[0:4, head * 65: head * 65 + 65],
                        vgl[0:4, head * 65: head * 65 + 65],
                        pph[h2][0:4, 384:512], ppl[h2][0:4, 384:512],
                        False, True, inc="t")
                tv = p.cnt["t"]
                for ch in cs:
                    ring_v[ch % RING].read_by("t")
                ppr.read_by("t")
                p.wait("vector", "t", tv)
                w_ao32.wait_free("vector")
                v_op(c, lambda e, o=ao32[0:64, 0:128], i=b[abank][0:64, 0:128]:
                     e.tensor_copy(o, i))
                ra_.read_by("v")
                w_ao32.write_by("v")
                sv = p.cnt["v"]
                p.wait("vector", "t", tv)
                w_rrow.wait_free("vector")
                w_p0.wait_free("vector")
                v_op(c, lambda e, o=p0[0:32, 0:128], i=b[abank][64:96, 0:128],
                     mk=tuple([0] * 32): e.stream_shuffle(o, i, list(mk)))
                ra_.read_by("v")
                v_op(c, lambda e, o=rrow, i=p0[0:1, 0:128]:
                     e.tensor_scalar(o, i, 1e-30, None, OP.add))
                v_op(c, lambda e, o=rrow, i=rrow: e.reciprocal(o, i))
                w_ao32.read_by("v")
                w_p0.write_by("v")
                w_p0.read_by("v")
                rb[7].wait_free("tensor")
                bcast16(c, b[7][0:64, 0:128], rrow, rr16h, rr16l,
                        c.sb["ones_r16"][0:1, 0:64])
                w_rrow.write_by("v")
                w_rrow.read_by("t")
                tv2 = p.cnt["t"]
                p.wait("vector", "t", tv2)
                ao_res.wait_free("vector")
                oh = aoh[h2 * 64:(h2 + 1) * 64, pair * GT + qcol: pair * GT + qcol + 128]
                ol = aol[h2 * 64:(h2 + 1) * 64, pair * GT + qcol: pair * GT + qcol + 128]
                ohs = nsc[0:64, 0:128]
                ols = nsc[0:64, 128:256]
                v_op(c, lambda e, o=ohs, aa=ao32[0:64, 0:128], bb=b[7][0:64, 0:128]:
                     e.tensor_tensor(o, aa, bb, OP.mult))
                v_op(c, lambda e, o=p0[0:64, 0:128], aa=ao32[0:64, 0:128],
                     bb=b[7][0:64, 0:128]: e.tensor_tensor(o, aa, bb, OP.mult))
                v_op(c, lambda e, o=ols, aa=p0[0:64, 0:128], bb=ohs:
                     e.tensor_tensor(o, aa, bb, OP.subtract))
                if h2 == 0:
                    v_op(c, lambda e, o=oh, i=ohs: e.tensor_copy(o, i))
                    v_op(c, lambda e, o=ol, i=ols: e.tensor_copy(o, i))
                else:
                    idm = list(range(32))
                    v_op(c, lambda e, o=oh, i=ohs, mk=tuple(idm):
                         e.stream_shuffle(o, i, list(mk)))
                    v_op(c, lambda e, o=ol, i=ols, mk=tuple(idm):
                         e.stream_shuffle(o, i, list(mk)))
                rb[7].read_by("v")
                w_ao32.read_by("v")
                w_p0.write_by("v")
                w_p0.read_by("v")
                ao_res.write_by("v")

    def wo_group(g, slot0, wsrc, bcol, into_x=True):
        si = slot0
        for ec in range(EC):
            wh, wl, slot = stream_w(c, wsrc[ec], si % 2)
            si += 1
            rb[0].wait_free("tensor")
            slot.wait_ready("tensor")
            ao_res.wait_ready("tensor")
            for dc in range(EC):
                mm3(c, b[0], wh[:, dc * 128:dc * 128 + 128],
                    wl[:, dc * 128:dc * 128 + 128],
                    aoh[:, dc * GT:(dc + 1) * GT],
                    aol[:, dc * GT:(dc + 1) * GT],
                    dc == 0, dc == EC - 1,
                    inc=("t" if dc == EC - 1 else None))
            slot.read_by("t")
            ao_res.read_by("t")
            p.wait("vector", "t")
            xc = xfm[:, ec * S + g * GT: ec * S + g * GT + GT]
            bslc = bias[:, bcol * EC + ec: bcol * EC + ec + 1]
            v_op(c, lambda e, o=xc, i=b[0], s1=bslc, x=xc:
                 e.scalar_tensor_tensor(o, i, s1, x, OP.add, OP.add))
            rb[0].read_by("v")
        return si

    def globrows_group(g, last_group):
        for pair in range(EC):
            for h2 in range(2):
                head = pair * 2 + h2
                qgh_s = qgh[h2 * 64:(h2 + 1) * 64, pair * 4:pair * 4 + 4]
                qgl_s = qgl[h2 * 64:(h2 + 1) * 64, pair * 4:pair * 4 + 4]
                sbanks = (1, 2) if h2 == 0 else (3, 4)
                c.misc["qg_res"].wait_ready("tensor")
                w_p0.wait_free("scalar")
                for tc in range(4):
                    sl = (g * 4 + tc) % RING
                    bk = sbanks[tc % 2]
                    rb[bk].wait_free("tensor")
                    ring_k[sl].wait_ready("tensor")
                    kb_h = krh[h2 * 64:(h2 + 1) * 64,
                               sl * D + pair * 128: sl * D + pair * 128 + 128]
                    kb_l = krl[h2 * 64:(h2 + 1) * 64,
                               sl * D + pair * 128: sl * D + pair * 128 + 128]
                    mm3(c, b[bk][:, 0:4], kb_h, kb_l, qgh_s, qgl_s, True, True,
                        inc="t")
                    ring_k[sl].read_by("t")
                    p.wait("scalar", "t")
                    s_op(c, lambda e, o=p0[:, tc * 4:(tc + 1) * 4], i=b[bk][:, 0:4]:
                         e.activation(o, i, AF.Exp, bias=c.sb["negC"], scale=SCL))
                    rb[bk].read_by("s")
                w_p0.write_by("s")
                sv = p.cnt["s"]
                ppr = pp_res[h2]
                ppr.wait_free("vector")
                p.wait("vector", "s", sv)
                v_op(c, lambda e, o=pph[h2][:, 0:16], i=p0[:, 0:16]: e.tensor_copy(o, i))
                v_op(c, lambda e, o=ppl[h2][:, 0:16], aa=p0[:, 0:16], bb=pph[h2][:, 0:16]:
                     e.tensor_tensor(o, aa, bb, OP.subtract))
                w_p0.read_by("v")
                ppr.write_by("v")
                vv = p.cnt["v"]
                rb[7].wait_free("tensor")
                p.wait("tensor", "v", vv)
                for tc in range(4):
                    sl = (g * 4 + tc) % RING
                    ring_v[sl].wait_ready("tensor")
                    vt_h = vrh[:, sl * 1040 + head * 65: sl * 1040 + head * 65 + 65]
                    vt_l = vrl[:, sl * 1040 + head * 65: sl * 1040 + head * 65 + 65]
                    mm3(c, b[7][0:65, 0:4], vt_h, vt_l,
                        pph[h2][:, tc * 4:(tc + 1) * 4],
                        ppl[h2][:, tc * 4:(tc + 1) * 4],
                        tc == 0, tc == 3, inc=("t" if tc == 3 else None))
                tv = p.cnt["t"]
                for tc in range(4):
                    ring_v[(g * 4 + tc) % RING].read_by("t")
                ppr.read_by("t")
                p.wait("vector", "t", tv)
                aga_res.wait_free("vector")
                v_op(c, lambda e, o=aga[0:65, head * 4:(head + 1) * 4],
                     aa=aga[0:65, head * 4:(head + 1) * 4], bb=b[7][0:65, 0:4]:
                     e.tensor_tensor(o, aa, bb, OP.add))
                rb[7].read_by("v")
                aga_res.write_by("v")

    def glob_finalize():
        aga_res.wait_ready("vector")
        w_rrow.wait_free("vector")
        v_op(c, lambda e, o=p0[0:32, 0:64], i=aga[64:96, 0:64], mk=tuple([0] * 32):
             e.stream_shuffle(o, i, list(mk)))
        rr = rrow[0:1, 0:64]
        v_op(c, lambda e, o=rr, i=p0[0:1, 0:64]:
             e.tensor_scalar(o, i, 1e-30, None, OP.add))
        v_op(c, lambda e, o=rr, i=rr: e.reciprocal(o, i))
        rb[7].wait_free("tensor")
        bcast16(c, b[7][0:64, 0:64], rr, rr16h[0:1, 0:64], rr16l[0:1, 0:64],
                c.sb["ones_r16"][0:1, 0:64])
        w_rrow.write_by("v")
        w_rrow.read_by("t")
        tv = p.cnt["t"]
        p.wait("vector", "t", tv)
        w_p0.wait_free("vector")
        v_op(c, lambda e, o=p0[0:64, 0:64], aa=aga[0:64, 0:64], bb=b[7][0:64, 0:64]:
             e.tensor_tensor(o, aa, bb, OP.mult))
        rb[7].read_by("v")
        aga_res.read_by("v")
        w_p0.write_by("v")
        for head in range(HEADS):
            src_ = p0[0:64, head * 4:(head + 1) * 4]
            r0 = 64 * (head % 2)
            cc = 4 * (head // 2)
            ohs = nsc[0:64, 0:4]
            ols = nsc[0:64, 4:8]
            v_op(c, lambda e, o=ohs, i=src_: e.tensor_copy(o, i))
            v_op(c, lambda e, o=ols, aa=src_, hh=ohs:
                 e.tensor_tensor(o, aa, hh, OP.subtract))
            if r0 == 0:
                v_op(c, lambda e, o=aogh[0:64, cc:cc + 4], i=ohs: e.tensor_copy(o, i))
                v_op(c, lambda e, o=aogl[0:64, cc:cc + 4], i=ols: e.tensor_copy(o, i))
            else:
                idm = tuple(range(32))
                v_op(c, lambda e, o=aogh[64:128, cc:cc + 4], i=ohs, mk=idm:
                     e.stream_shuffle(o, i, list(mk)))
                v_op(c, lambda e, o=aogl[64:128, cc:cc + 4], i=ols, mk=idm:
                     e.stream_shuffle(o, i, list(mk)))
        w_p0.read_by("v")
        aog_res = Res(p)
        aog_res.write_by("v")
        si = 0
        for ec in range(EC):
            wh, wl, slot = stream_w(c, c.dram["wo"][ec], si % 2)
            si += 1
            rb[0].wait_free("tensor")
            slot.wait_ready("tensor")
            aog_res.wait_ready("tensor")
            for dc in range(EC):
                mm3(c, b[0][:, 0:4], wh[:, dc * 128:dc * 128 + 128],
                    wl[:, dc * 128:dc * 128 + 128],
                    aogh[:, dc * 4:(dc + 1) * 4],
                    aogl[:, dc * 4:(dc + 1) * 4],
                    dc == 0, dc == EC - 1,
                    inc=("t" if dc == EC - 1 else None))
            slot.read_by("t")
            aog_res.read_by("t")
            p.wait("vector", "t")
            xc = xfm[:, ec * S: ec * S + 4]
            v_op(c, lambda e, o=xc, aa=xc, bb=b[0][:, 0:4]:
                 e.tensor_tensor(o, aa, bb, OP.add))
            rb[0].read_by("v")

    for g in range(NG):
        layer_norm_group(c, g, wk)
        if DEV_STAGE < 2:
            continue
        si = proj_group(g, [("k", c.dram["wk"], 1), ("v", c.dram["wv"], 2)], 0)
        if g == 0:
            for pair in range(EC):
                ring_k[0].wait_ready("vector")
                v_op(c, lambda e, o=kgh[:, pair * 4:pair * 4 + 4],
                     i=krh[:, pair * 128: pair * 128 + 4]: e.tensor_copy(o, i))
                v_op(c, lambda e, o=kgl[:, pair * 4:pair * 4 + 4],
                     i=krl[:, pair * 128: pair * 128 + 4]: e.tensor_copy(o, i))
                ring_k[0].read_by("v")
            ring_v[0].wait_ready("vector")
            v_op(c, lambda e, o=vgh[0:4, :], i=vrh[0:4, 0:1040]: e.tensor_copy(o, i))
            v_op(c, lambda e, o=vgl[0:4, :], i=vrl[0:4, 0:1040]: e.tensor_copy(o, i))
            ring_v[0].read_by("v")
            for nm in ("kg_res", "vg_res"):
                r = Res(p)
                r.write_by("v")
                c.misc[nm] = r
        c.misc["ntiles"] = c.misc.get("ntiles", 0)
        if g > 0 and DEV_STAGE >= 3 and c.misc["ntiles"] < DEV_NTILES:
            c.misc["ntiles"] += 1
            attn_tile(4 * g - 1)
            if DEV_STAGE >= 4:
                si = wo_group(g - 1, si, c.dram["wo"], 3)
        si = proj_group(g, [("q", c.dram["wq"], 0)], si)
        if g == 0:
            q_res.wait_ready("vector")
            for pair in range(EC):
                v_op(c, lambda e, o=qgh[:, pair * 4:pair * 4 + 4],
                     i=qh[:, pair * GT:pair * GT + 4]: e.tensor_copy(o, i))
                v_op(c, lambda e, o=qgl[:, pair * 4:pair * 4 + 4],
                     i=ql[:, pair * GT:pair * GT + 4]: e.tensor_copy(o, i))
            q_res.read_by("v")
            r = Res(p)
            r.write_by("v")
            c.misc["qg_res"] = r
        if DEV_STAGE >= 3:
            for qt in range(4 * g, 4 * g + 3):
                if c.misc["ntiles"] < DEV_NTILES:
                    c.misc["ntiles"] += 1
                    attn_tile(qt)
        if DEV_STAGE >= 5:
            globrows_group(g, g == NG - 1)
    if DEV_STAGE >= 3 and c.misc.get("ntiles", 0) < DEV_NTILES:
        attn_tile(NQT - 1)
    if DEV_STAGE >= 4:
        wo_group(NG - 1, 0, c.dram["wo"], 3)
    if DEV_STAGE >= 5:
        glob_finalize()
    a.release(m0)
    for nm in ("kg_res", "vg_res", "qg_res"):
        c.misc.pop(nm, None)



def sub_ffn(c, seq):
    p = c.p
    a = c.arena
    m0 = a.mark()
    xfm = c.sb["xfm"]
    b = c.psum
    rb = c.bank

    hh = a.take(HC * GT, F16)
    hl = a.take(HC * GT, F16)
    wg16 = a.take(2 * D, F16)
    wu16 = a.take(2 * D, F16)
    wd16 = a.take(2 * FH, F16)
    wk = ln_wk(c, a)
    silu32 = a.take(GT, F32)
    h32 = a.take(GT, F32)

    g_res = [Res(p), Res(p)]
    u_res = [Res(p), Res(p)]
    d_res = [Res(p), Res(p)]
    h_res = Res(p)
    w_silu = Res(p)
    w_h32 = Res(p)

    for g in range(NG):
        layer_norm_group(c, g, wk)
        for hc in range(HC):
            si = hc % 2
            ldg = dma(c, "sync", "lg" + str(si), wg16[:, si * D:(si + 1) * D],
                      c.dram["scr_g"][hc], res_list=[g_res[si]])
            ldu = dma(c, "sync", "lu" + str(si), wu16[:, si * D:(si + 1) * D],
                      c.dram["scr_u"][hc], res_list=[u_res[si]])
            p.wait("tensor", "lg" + str(si), ldg)
            p.wait("tensor", "lu" + str(si), ldu)
            rb[0].wait_free("tensor")
            rb[1].wait_free("tensor")
            c.xn_res.wait_ready("tensor")
            for dc in range(EC):
                mm2(c, b[0], wg16[:, si * D + dc * 128: si * D + dc * 128 + 128],
                    c.sb["xnh"][:, dc * GT:(dc + 1) * GT],
                    c.sb["xnl"][:, dc * GT:(dc + 1) * GT],
                    dc == 0, dc == EC - 1,
                    inc=("t" if dc == EC - 1 else None))
            for dc in range(EC):
                mm2(c, b[1], wu16[:, si * D + dc * 128: si * D + dc * 128 + 128],
                    c.sb["xnh"][:, dc * GT:(dc + 1) * GT],
                    c.sb["xnl"][:, dc * GT:(dc + 1) * GT],
                    dc == 0, dc == EC - 1,
                    inc=("t" if dc == EC - 1 else None))
            g_res[si].read_by("t")
            u_res[si].read_by("t")
            c.xn_res.read_by("t")
            tv = p.cnt["t"]
            p.wait("scalar", "t", tv)
            w_silu.wait_free("scalar")
            s_op(c, lambda e, o=silu32, i=b[0]:
                 e.activation(o, i, AF.Silu, bias=0.0, scale=FS))
            rb[0].read_by("s")
            w_silu.write_by("s")
            sv = p.cnt["s"]
            p.wait("vector", "s", sv)
            p.wait("vector", "t", tv)
            w_h32.wait_free("vector")
            v_op(c, lambda e, o=h32, aa=silu32, bb=b[1]: e.tensor_tensor(o, aa, bb, OP.mult))
            rb[1].read_by("v")
            w_silu.read_by("v")
            w_h32.write_by("v")
            h_res.wait_free("vector")
            oh = hh[:, hc * GT:(hc + 1) * GT]
            ol = hl[:, hc * GT:(hc + 1) * GT]
            v_op(c, lambda e, o=oh, i=h32: e.tensor_scalar(o, i, H_SH, None, OP.mult))
            v_op(c, lambda e, o=ol, i=h32, hh_=oh:
                 e.scalar_tensor_tensor(o, i, H_SH, hh_, OP.mult, OP.subtract))
            w_h32.read_by("v")
            h_res.write_by("v")
        for dc in range(EC):
            si = dc % 2
            ldd = dma(c, "sync", "ldn" + str(si), wd16[:, si * FH:(si + 1) * FH],
                      c.dram["scr_d"][dc], res_list=[d_res[si]])
            p.wait("tensor", "ldn" + str(si), ldd)
            rb[2].wait_free("tensor")
            h_res.wait_ready("tensor")
            for hc in range(HC):
                mm2(c, b[2], wd16[:, si * FH + hc * 128: si * FH + hc * 128 + 128],
                    hh[:, hc * GT:(hc + 1) * GT],
                    hl[:, hc * GT:(hc + 1) * GT],
                    hc == 0, hc == HC - 1,
                    inc=("t" if hc == HC - 1 else None))
            d_res[si].read_by("t")
            h_res.read_by("t")
            p.wait("vector", "t")
            xc = xfm[:, dc * S + g * GT: dc * S + g * GT + GT]
            v_op(c, lambda e, o=xc, i=b[2], x=xc:
                 e.scalar_tensor_tensor(o, i, FFN_OUT_SCALE, x, OP.mult, OP.add))
            rb[2].read_by("v")
    a.release(m0)



def sub_attn_b(c, seq):
    p = c.p
    a = c.arena
    m0 = a.mark()
    xfm = c.sb["xfm"]
    b = c.psum
    rb = c.bank
    bias = c.sb["bias"]

    wk = ln_wk(c, a)
    for g in range(NG):
        layer_norm_group(c, g, wk)
        c.xn_res.wait_ready("scalar")
        dma(c, "scalar", "st", c.dram["scr_xnh"][g], c.sb["xnh"])
        dma(c, "scalar", "st", c.dram["scr_xnl"][g], c.sb["xnl"])
        c.xn_res.read_by("st")

    rc = a.take(S, F32)
    rs = a.take(S, F32)
    ld1 = dma(c, "sync", "lx", rc, c.dram["rc2"])
    ld2 = dma(c, "sync", "lx", rs, c.dram["rs2"])
    p.wait("gpsimd", "lx", ld2)

    qth = a.take(GT, F16); qtl = a.take(GT, F16)
    qrh = a.take(GT, F16); qrl = a.take(GT, F16)
    kfh = a.take(S, F16);  kfl = a.take(S, F16)
    krh = a.take(S, F16);  krl = a.take(S, F16)
    vth = a.take(S, F16);  vtl = a.take(S, F16)
    wpair = [a.take(D, F16) for _ in range(6)]
    q32 = a.take(GT, F32)
    sw32 = a.take(GT, F32)
    ra32 = a.take(GT, F32)
    rb32 = a.take(GT, F32)
    p0 = a.take(GT, F32)
    eju = a.take(GT, F16)
    den4 = a.take(8, F32)
    den_all = a.take(8, F32)
    denr = a.take(GT, F32)[0:1, :]
    dr16h = a.take(GT, BF16)[0:1, :]
    dr16l = a.take(GT, BF16)[0:1, :]
    ao32 = a.take(GT, F32)
    aoth = a.take(S, F16)
    aotl = a.take(S, F16)
    pph = [a.take(GT, BF16) for _ in range(2)]
    ppl = [a.take(GT, BF16) for _ in range(2)]

    q_res = Res(p); qr_res = Res(p); k_res = Res(p); kr_res = Res(p); vt_res = Res(p)
    w_res = Res(p)
    w_q32 = Res(p); w_sw = Res(p); w_r32 = Res(p); w_p0 = Res(p)
    pp_res = [Res(p), Res(p)]
    w_den4 = Res(p); w_denall = Res(p); w_denr = Res(p)
    ao_res = Res(p)
    w_ao32 = Res(p)

    for pair in range(EC):
        for mi, ecw in enumerate([pair, EC + pair, 2 * EC + pair]):
            w32 = c.sb["wb32"]
            w32_res = c.misc.setdefault("w32_res", Res(p))
            ldv = dma(c, "sync", "lw", w32, c.dram["wqkv"][ecw], res_list=[w32_res])
            w32_res.write_by("lw")
            p.wait("vector", "lw", ldv)
            if mi == 0:
                w_res.wait_free("vector")
            v_op(c, lambda e, o=wpair[2 * mi], i=w32: e.tensor_copy(o, i))
            v_op(c, lambda e, o=wpair[2 * mi + 1], aa=w32, bb=wpair[2 * mi]:
                 e.tensor_tensor(o, aa, bb, OP.subtract))
            w32_res.read_by("v")
        w_res.write_by("v")

        for g in range(NG):
            c.xn_res.wait_free("sync")
            p.wait("sync", "st")
            dma(c, "sync", "lx", c.sb["xnh"], c.dram["scr_xnh"][g])
            ldl = dma(c, "sync", "lx", c.sb["xnl"], c.dram["scr_xnl"][g])
            c.xn_res.ready = ("lx", ldl)
            c.xn_res.wait_ready("tensor")
            for mi, mat in enumerate(["q", "k", "v"]):
                wh_, wl_ = wpair[2 * mi], wpair[2 * mi + 1]
                w_res.wait_ready("tensor")
                rb[2].wait_free("tensor")
                for dc in range(EC):
                    mm3(c, b[2], wh_[:, dc * 128:dc * 128 + 128],
                        wl_[:, dc * 128:dc * 128 + 128],
                        c.sb["xnh"][:, dc * GT:(dc + 1) * GT],
                        c.sb["xnl"][:, dc * GT:(dc + 1) * GT],
                        dc == 0, dc == EC - 1,
                        inc=("t" if dc == EC - 1 else None))
                c.xn_res.read_by("t")
                w_res.read_by("t")
                tv = p.cnt["t"]
                if mat == "v":
                    p.wait("scalar", "t", tv)
                    w_q32.wait_free("scalar")
                    s_op(c, lambda e, o=q32, i=b[2]:
                         e.activation(o, i, AF.Copy, bias=0.0, scale=1.0))
                    rb[2].read_by("s")
                    w_q32.write_by("s")
                    for tc in range(4):
                        ch = g * 4 + tc
                        rb[1].wait_free("tensor")
                        w_q32.wait_ready("tensor")
                        c.p.op("tensor", lambda e, o=b[1][:, 0:128],
                               i=q32[:, tc * 128:(tc + 1) * 128], idn=c.sb["id32"]:
                               e.transpose(o, i, idn), inc="t")
                        w_q32.read_by("t")
                        p.wait("vector", "t")
                        vt_res.wait_free("vector")
                        oh = vth[:, ch * 128:(ch + 1) * 128]
                        ol = vtl[:, ch * 128:(ch + 1) * 128]
                        v_op(c, lambda e, o=oh, i=b[1][:, 0:128]: e.tensor_copy(o, i))
                        v_op(c, lambda e, o=ol, aa=b[1][:, 0:128], bb=oh:
                             e.tensor_tensor(o, aa, bb, OP.subtract))
                        rb[1].read_by("v")
                        vt_res.write_by("v")
                    continue
                p.wait("scalar", "t", tv)
                w_q32.wait_free("scalar")
                s_op(c, lambda e, o=q32, i=b[2]:
                     e.activation(o, i, AF.Copy, bias=0.0, scale=1.0))
                rb[2].read_by("s")
                w_q32.write_by("s")
                sv = p.cnt["s"]
                p.wait("vector", "s", sv)
                w_sw.wait_free("vector")
                v_op(c, lambda e, o=sw32, i=q32, mk=tuple(SWAP_MASK):
                     e.stream_shuffle(o, i, list(mk)))
                w_q32.read_by("v")
                w_sw.write_by("v")
                vv = p.cnt["v"]
                p.wait("gpsimd", "v", vv)
                p.wait("gpsimd", "s", sv)
                w_r32.wait_free("gpsimd")
                g_op(c, lambda e, o=ra32, aa=q32, bb=rc[:, g * GT:(g + 1) * GT]:
                     e.tensor_tensor(o, aa, bb, OP.mult))
                g_op(c, lambda e, o=rb32, aa=sw32, bb=rs[:, g * GT:(g + 1) * GT]:
                     e.tensor_tensor(o, aa, bb, OP.mult))
                w_q32.read_by("g")
                w_sw.read_by("g")
                w_r32.write_by("g")
                gv = p.cnt["g"]
                if mat == "q":
                    dh_, dl_ = qth, qtl
                    drh_, drl_ = qrh, qrl
                    res_p, res_r = q_res, qr_res
                    offp = 0
                else:
                    dh_, dl_ = kfh, kfl
                    drh_, drl_ = krh, krl
                    res_p, res_r = k_res, kr_res
                    offp = g * GT
                res_p.wait_free("vector")
                v_op(c, lambda e, o=dh_[:, offp:offp + GT], i=q32: e.tensor_copy(o, i))
                v_op(c, lambda e, o=dl_[:, offp:offp + GT], aa=q32,
                     bb=dh_[:, offp:offp + GT]: e.tensor_tensor(o, aa, bb, OP.subtract))
                w_q32.read_by("v")
                res_p.write_by("v")
                p.wait("vector", "g", gv)
                res_r.wait_free("vector")
                v_op(c, lambda e, o=ra32, aa=ra32, bb=rb32: e.tensor_tensor(o, aa, bb, OP.add))
                v_op(c, lambda e, o=drh_[:, offp:offp + GT], i=ra32: e.tensor_copy(o, i))
                v_op(c, lambda e, o=drl_[:, offp:offp + GT], aa=ra32,
                     bb=drh_[:, offp:offp + GT]: e.tensor_tensor(o, aa, bb, OP.subtract))
                w_r32.read_by("v")
                res_r.write_by("v")

            qt5 = g
            nks = g + 1
            for sub in range(4):
                qt = qt5 * 4 + sub
                w_den4.wait_free("scalar")
                for h2 in range(2):
                    sb_ = 1 + h2
                    q_res.wait_ready("tensor")
                    k_res.wait_ready("tensor")
                    for ks in range(nks):
                        rb[sb_].wait_free("tensor")
                        qhs = qth[h2 * 64:(h2 + 1) * 64, sub * 128: sub * 128 + 128]
                        qls = qtl[h2 * 64:(h2 + 1) * 64, sub * 128: sub * 128 + 128]
                        khs = kfh[h2 * 64:(h2 + 1) * 64, ks * 512:(ks + 1) * 512]
                        kls = kfl[h2 * 64:(h2 + 1) * 64, ks * 512:(ks + 1) * 512]
                        mm3(c, b[sb_], qhs, qls, khs, kls, True, True, inc="t")
                        q_res.read_by("t")
                        k_res.read_by("t")
                        tv = p.cnt["t"]
                        p.wait("scalar", "t", tv)
                        s_op(c, lambda e, o=eju, i=b[sb_], acc=den4[:, (h2 * 4 + ks):(h2 * 4 + ks) + 1]:
                             e.activation(o, i, AF.Exp, bias=c.sb["negC"][0:i.shape[0], :], scale=SCL, accum_out=acc))
                        rb[sb_].read_by("s")
                w_den4.write_by("s")
                sv = p.cnt["s"]
                p.wait("vector", "s", sv)
                w_denall.wait_free("vector")
                for h2 in range(2):
                    csl = den_all[:, (h2 * 4 + sub):(h2 * 4 + sub) + 1]
                    v_op(c, lambda e, o=csl, i=den4[:, h2 * 4: h2 * 4 + 1]:
                         e.tensor_copy(o, i))
                    for ks in range(1, nks):
                        v_op(c, lambda e, o=csl, aa=csl,
                             bb=den4[:, (h2 * 4 + ks):(h2 * 4 + ks) + 1]:
                             e.tensor_tensor(o, aa, bb, OP.add))
                w_den4.read_by("v")
                w_denall.write_by("v")
            for h2 in range(2):
                bk = 5 + h2
                rb[bk].wait_free("tensor")
                w_denall.wait_ready("tensor")
                for sub in range(4):
                    c.p.op("tensor", lambda e, o=b[bk][0:1, sub * 128:(sub + 1) * 128],
                           i=den_all[:, (h2 * 4 + sub):(h2 * 4 + sub) + 1],
                           idn=c.sb["id32"]: e.transpose(o, i, idn),
                           inc=("t" if sub == 3 else None))
                w_denall.read_by("t")
            tv = p.cnt["t"]
            p.wait("vector", "t", tv)
            rb[3].wait_free("tensor")
            rb[4].wait_free("tensor")
            nch_q = 4 * qt5 + 4
            for h2 in range(2):
                sb_ = 1 + h2
                for ci in range(nch_q):
                    rb[sb_].wait_free("tensor")
                    kr_res.wait_ready("tensor")
                    qr_res.wait_ready("tensor")
                    khs = krh[h2 * 64:(h2 + 1) * 64, ci * 128:(ci + 1) * 128]
                    kls = krl[h2 * 64:(h2 + 1) * 64, ci * 128:(ci + 1) * 128]
                    qhs = qrh[h2 * 64:(h2 + 1) * 64, :]
                    qls = qrl[h2 * 64:(h2 + 1) * 64, :]
                    mm3(c, b[sb_], khs, kls, qhs, qls, True, True, inc="t")
                    kr_res.read_by("t")
                    qr_res.read_by("t")
                    tv = p.cnt["t"]
                    p.wait("scalar", "t", tv)
                    w_p0.wait_free("scalar")
                    s_op(c, lambda e, o=p0, i=b[sb_]:
                         e.activation(o, i, AF.Exp, bias=c.sb["negC"][0:i.shape[0], :], scale=SCL))
                    rb[sb_].read_by("s")
                    w_p0.write_by("s")
                    sv = p.cnt["s"]
                    off = ci - 4 * qt5
                    if off >= 0:
                        p.wait("gpsimd", "s", sv)
                        o_ = 128 * off
                        g_op(c, lambda e, o=p0, aa=p0,
                             bb=c.sb["m_b"][:, 384 - o_: 896 - o_]:
                             e.tensor_tensor(o, aa, bb, OP.mult))
                        w_p0.write_by("g")
                        p.wait("vector", "g")
                    else:
                        p.wait("vector", "s", sv)
                    pp_res[h2].wait_free("vector")
                    v_op(c, lambda e, o=pph[h2], i=p0: e.tensor_copy(o, i))
                    v_op(c, lambda e, o=ppl[h2], aa=p0, bb=pph[h2]:
                         e.tensor_tensor(o, aa, bb, OP.subtract))
                    w_p0.read_by("v")
                    pp_res[h2].write_by("v")
                    vv = p.cnt["v"]
                    p.wait("tensor", "v", vv)
                    vt_res.wait_ready("tensor")
                    vt_h = vth[:, ci * 128 + h2 * 64: ci * 128 + h2 * 64 + 64]
                    vt_l = vtl[:, ci * 128 + h2 * 64: ci * 128 + h2 * 64 + 64]
                    mm3(c, b[3 + h2][h2 * 64:(h2 + 1) * 64, :], vt_h, vt_l,
                        pph[h2], ppl[h2], ci == 0, ci == nch_q - 1, inc="t")
                    vt_res.read_by("t")
                    pp_res[h2].read_by("t")
            tv_av = p.cnt["t"]
            p.wait("scalar", "t", tv_av)
            w_ao32.wait_free("scalar")
            s_op(c, lambda e, o=ao32[0:64, :], i=b[3][0:64, :]:
                 e.activation(o, i, AF.Copy, bias=0.0, scale=1.0))
            s_op(c, lambda e, o=ao32[64:128, :], i=b[4][64:128, :]:
                 e.activation(o, i, AF.Copy, bias=0.0, scale=1.0))
            rb[3].read_by("s")
            rb[4].read_by("s")
            w_ao32.write_by("s")
            sv = p.cnt["s"]
            p.wait("vector", "s", sv)
            ao_res.wait_free("vector")
            for h2 in range(2):
                bb_ = 6 if h2 == 0 else 7
                for sub in range(4):
                    rb[5].wait_free("tensor")
                    w_denall.wait_ready("tensor")
                    c.p.op("tensor", lambda e, o=b[5][0:1, 0:128],
                           i=den_all[:, (h2 * 4 + sub):(h2 * 4 + sub) + 1],
                           idn=c.sb["id32"]: e.transpose(o, i, idn), inc="t")
                    w_denall.read_by("t")
                    p.wait("vector", "t")
                    w_denr.wait_free("vector")
                    v_op(c, lambda e, o=denr[0:1, 0:128], i=b[5][0:1, 0:128]:
                         e.reciprocal(o, i))
                    rb[5].read_by("v")
                    rb[bb_].wait_free("tensor")
                    bcast16(c, b[bb_][h2 * 64:(h2 + 1) * 64, 0:128],
                            denr[0:1, 0:128], dr16h[0:1, 0:128], dr16l[0:1, 0:128],
                            c.sb["ones_r16"][0:1, 0:64])
                    w_denr.write_by("v")
                    w_denr.read_by("t")
                    tv_bc = p.cnt["t"]
                    p.wait("vector", "t", tv_bc)
                    r0 = h2 * 64
                    qq = qt5 * GT + sub * 128
                    oh = aoth[r0:r0 + 64, qq:qq + 128]
                    ol = aotl[r0:r0 + 64, qq:qq + 128]
                    a32 = ao32[r0:r0 + 64, sub * 128:(sub + 1) * 128]
                    bc = b[bb_][r0:r0 + 64, 0:128]
                    w_p0.wait_free("vector")
                    v_op(c, lambda e, o=oh, aa=a32, bb=bc:
                         e.tensor_tensor(o, aa, bb, OP.mult))
                    v_op(c, lambda e, o=p0[r0:r0 + 64, 0:128], aa=a32, bb=bc:
                         e.tensor_tensor(o, aa, bb, OP.mult))
                    v_op(c, lambda e, o=ol, aa=p0[r0:r0 + 64, 0:128], bb=oh:
                         e.tensor_tensor(o, aa, bb, OP.subtract))
                    rb[bb_].read_by("v")
                    w_p0.write_by("v")
                    w_p0.read_by("v")
            w_ao32.read_by("v")
            ao_res.write_by("v")
        ao_res.wait_ready("scalar")
        dma(c, "scalar", "st", c.dram["scr_aoh"][pair], aoth)
        dma(c, "scalar", "st", c.dram["scr_aol"][pair], aotl)
        ao_res.read_by("st")

    a.release(m0)
    m1 = a.mark()
    aobh = a.take(EC * GT, F16)
    aobl = a.take(EC * GT, F16)
    aob_res = Res(p)
    stv = p.cnt["st"]
    p.wait("sync", "st", stv)
    for g in range(NG):
        aob_res.wait_free("sync")
        for dc in range(EC):
            dma(c, "sync", "lx", aobh[:, dc * GT:(dc + 1) * GT],
                c.dram["scr_aoh"][dc][:, g * GT:(g + 1) * GT])
            ldl = dma(c, "sync", "lx", aobl[:, dc * GT:(dc + 1) * GT],
                      c.dram["scr_aol"][dc][:, g * GT:(g + 1) * GT])
        aob_res.ready = ("lx", p.cnt["lx"])
        si = 0
        for ec in range(EC):
            wh, wl, slot = stream_w(c, c.dram["wout"][ec], si % 2)
            si += 1
            rb[0].wait_free("tensor")
            slot.wait_ready("tensor")
            aob_res.wait_ready("tensor")
            for dc in range(EC):
                mm3(c, b[0], wh[:, dc * 128:dc * 128 + 128],
                    wl[:, dc * 128:dc * 128 + 128],
                    aobh[:, dc * GT:(dc + 1) * GT],
                    aobl[:, dc * GT:(dc + 1) * GT],
                    dc == 0, dc == EC - 1,
                    inc=("t" if dc == EC - 1 else None))
            slot.read_by("t")
            aob_res.read_by("t")
            p.wait("vector", "t")
            xc = xfm[:, ec * S + g * GT: ec * S + g * GT + GT]
            bslc = bias[:, 4 * EC + ec: 4 * EC + ec + 1]
            v_op(c, lambda e, o=xc, i=b[0], s1=bslc, x=xc:
                 e.scalar_tensor_tensor(o, i, s1, x, OP.add, OP.add))
            rb[0].read_by("v")
    a.release(m1)



def _prep_w(w):
    n_in, n_out = w.shape
    dc, ec = n_in // 128, n_out // 128
    t = w.reshape(dc, 128, ec, 128).transpose(2, 1, 0, 3)
    return np.ascontiguousarray(t.reshape(ec, 128, n_in))


def _win_rel(o):
    k = np.arange(128)[:, None] + o
    q = np.arange(128)[None, :]
    return np.abs(k - q) <= 32


def _prep_shared(inputs):
    f16 = ml_dtypes.float16 if hasattr(ml_dtypes, "float16") else np.float16
    sh = {}
    sh["wq"] = _prep_w(np.ascontiguousarray(inputs["L0_attn_a_Wq"].T))
    sh["wk"] = _prep_w(np.ascontiguousarray(inputs["L0_attn_a_Wk"].T))
    sh["wv"] = _prep_w(np.ascontiguousarray(inputs["L0_attn_a_Wv"].T))
    sh["wo"] = _prep_w(np.ascontiguousarray(inputs["L0_attn_a_Wo"].T))
    sh["wqkv"] = _prep_w(np.ascontiguousarray(inputs["L0_attn_b_Wqkv"]))
    sh["wout"] = _prep_w(np.ascontiguousarray(inputs["L0_attn_b_Wout"]))
    sh["wgi"] = _prep_w(np.ascontiguousarray(inputs["L0_ffn_w_gate"].T))
    sh["wui"] = _prep_w(np.ascontiguousarray(inputs["L0_ffn_w_up"].T))
    sh["wdi"] = _prep_w(np.ascontiguousarray(inputs["L0_ffn_w_down"].T))
    bs = [np.asarray(inputs["L0_attn_a_b" + n], np.float32) for n in "qkvo"]
    bs.append(np.asarray(inputs["L0_attn_b_bout"], np.float32))
    sh["biases"] = np.ascontiguousarray(
        np.concatenate([bb.reshape(EC, 128).T for bb in bs], axis=1))
    kk = np.arange(128)[:, None]
    qq = np.arange(128)[None, :]
    mid = np.concatenate([_win_rel(-128), _win_rel(0), _win_rel(128)], axis=1)
    t0b0 = _win_rel(0) | (kk < 4)
    t0 = np.concatenate([t0b0, _win_rel(128)], axis=1).astype(np.float32)
    t0[:, 0:4] = 0.0
    t0[:, 256:260] = 0.0
    t1b0 = _win_rel(-128) | (kk < 4)
    t1 = np.concatenate([t1b0, _win_rel(0), _win_rel(128)], axis=1)
    sh["m_mid"] = mid.astype(f16)
    sh["m_t0"] = t0.astype(f16)
    sh["m_t1"] = t1.astype(np.float32).astype(f16)
    qq896 = np.arange(896)[None, :]
    sh["m_b"] = (kk <= qq896 - 384).astype(np.float32).astype(f16)
    i = np.arange(32, dtype=np.float32)
    theta = np.power(np.float32(10000.0), (-2.0 * (i - 1.0) / 64.0).astype(np.float32))
    ang = (np.arange(S, dtype=np.float32)[:, None] * theta[None, :]).astype(np.float32)
    cos = np.cos(ang).astype(np.float32)
    sin = np.sin(ang).astype(np.float32)
    rc = np.zeros((64, S), np.float32)
    rc[0::2] = cos.T
    rc[1::2] = cos.T
    rs = np.zeros((64, S), np.float32)
    rs[0::2] = -sin.T
    rs[1::2] = sin.T
    sh["rc2"] = np.ascontiguousarray(np.tile(rc, (2, 1)))
    sh["rs2"] = np.ascontiguousarray(np.tile(rs, (2, 1)))
    sh["id32"] = np.eye(128, dtype=np.float32)
    return sh


def _x_to_fm(xs):
    return np.ascontiguousarray(
        xs.T.reshape(EC, 128, xs.shape[0]).transpose(1, 0, 2).reshape(128, -1))


def _fm_to_x(y, T):
    return np.ascontiguousarray(
        y.reshape(128, EC, T).transpose(1, 0, 2).reshape(D, T).T)


_NC_CACHE = {}


def kernel(**inputs):
    n_sub = inputs.pop("_n_sub", N_SUB)
    cores = inputs.pop("_cores", 8)
    x = np.asarray(inputs["L0_attn_a_x"], np.float32)
    sh = _prep_shared(inputs)
    key = n_sub
    if key not in _NC_CACHE:
        _NC_CACHE[key] = build_program(n_sub)
    nc = _NC_CACHE[key]
    in_maps = []
    for core in range(cores):
        m = dict(sh)
        xs = x[2 * core: 2 * core + 2]
        m["x"] = np.stack([_x_to_fm(xs[0]), _x_to_fm(xs[1])])
        in_maps.append(m)
    res = run_bass_kernel_spmd(nc, in_maps, core_ids=list(range(cores)))
    outs = []
    for core in range(cores):
        y = res.results[core]["y"]
        outs.append(np.stack([_fm_to_x(y[0], S), _fm_to_x(y[1], S)]))
    full = np.concatenate(outs, axis=0)
    if cores < 8:
        full = np.concatenate([full] + [np.zeros_like(outs[0])] * (8 - cores), 0)
    return full.astype(np.float32)
